# version 19
# baseline (speedup 1.0000x reference)
import numpy as np
import contextlib
import concourse.bass as bass
import concourse.mybir as mybir
import concourse.bass_utils as _bass_utils
from concourse.bass_utils import run_bass_kernel_spmd


F32 = mybir.dt.float32
BF16 = mybir.dt.bfloat16
I8 = mybir.dt.int8
AF = mybir.ActivationFunctionType
ALU = mybir.AluOpType

NCORES = 8
TC = 2048
NT = 4
TT = 512
NS = 4
C = 2048
CCN = 16
INTER = 8192
ICN = 64
OQKV = 6144
LN_EPS = 1e-5
ATT_EPS = 1e-15

ENGS = ("tensor", "scalar", "vector", "gpsimd", "sync")
QSEM = {"sync": "dsp", "scalar": "dsc", "gpsimd": "dgp"}


class Em:
    def __init__(self, sem_factory=None):
        self.streams = {k: [] for k in ENGS}
        self.cnt = {}
        self.hdl = {}
        self.sem_factory = sem_factory

    def sem(self, name, h):
        self.hdl[name] = h
        self.cnt[name] = 0

    def ensure(self, name):
        if name not in self.hdl:
            self.sem(name, self.sem_factory(name))

    def w(self, eng, sname, val):
        if val is None or val <= 0:
            return
        assert val <= self.cnt[sname], (eng, sname, val, self.cnt[sname])
        h = self.hdl[sname]
        self.streams[eng].append(lambda e, h=h, v=val: e.wait_ge(h, v))

    def wr(self, eng, rv):
        if rv is not None:
            self.w(eng, rv[0], rv[1])

    def i(self, eng, f, sname=None, n=1):
        if sname is None:
            self.streams[eng].append(f)
            return None
        h = self.hdl[sname]
        self.cnt[sname] += n
        self.streams[eng].append(lambda e, f=f, h=h, n=n: f(e).then_inc(h, n))
        return self.cnt[sname]

    def dma(self, q, sname, out_ap, in_ap, waits=()):
        self.ensure(sname)
        for rv in waits:
            self.wr(q, rv)
        return self.i(q, lambda e, o=out_ap, i=in_ap: e.dma_start(out=o, in_=i),
                      sname, 16)

    def dmat(self, q, sname, out_ap, in_ap, waits=()):
        self.ensure(sname)
        for rv in waits:
            self.wr(q, rv)
        return self.i(q, lambda e, o=out_ap, i=in_ap: e.dma_start_transpose(o, i),
                      sname, 16)


def build(stage=3):
    nc = bass.Bass("TRN2", target_bir_lowering=False)
    st = contextlib.ExitStack()

    x_h = nc.dram_tensor("x", [TC, C], F32, kind="ExternalInput")
    wqkv_h = nc.dram_tensor("wqkv", [OQKV, C], F32, kind="ExternalInput")
    wproj_h = nc.dram_tensor("wproj", [C, C], F32, kind="ExternalInput")
    bproj_h = nc.dram_tensor("bproj", [C], F32, kind="ExternalInput")
    wg_h = nc.dram_tensor("wg", [C, INTER], I8, kind="ExternalInput")
    wu_h = nc.dram_tensor("wu", [C, INTER], I8, kind="ExternalInput")
    wd_h = nc.dram_tensor("wd", [INTER, C], I8, kind="ExternalInput")
    sg_h = nc.dram_tensor("sg", [INTER], F32, kind="ExternalInput")
    su_h = nc.dram_tensor("su", [INTER], F32, kind="ExternalInput")
    sd_h = nc.dram_tensor("sd", [C], F32, kind="ExternalInput")
    out_h = nc.dram_tensor("out", [TC, C], F32, kind="ExternalOutput")

    wqkvT = nc.dram_tensor("wqkvT", [C, OQKV], BF16)
    wprojT = nc.dram_tensor("wprojT", [C, C], BF16)
    wqp_bf = nc.dram_tensor("wqp_bf", [OQKV + C, C], BF16)
    xp_d = nc.dram_tensor("xp_d", [TC, C], F32)
    q_d = nc.dram_tensor("q_d", [C, TC], BF16)
    yA_d = nc.dram_tensor("yA_d", [TC, C], BF16)
    yB_d = nc.dram_tensor("yB_d", [TC, C], BF16)
    yC_d = nc.dram_tensor("yC_d", [TC, C], BF16)
    dP_d = nc.dram_tensor("dP_d", [C, TC], BF16)
    att_d = nc.dram_tensor("att_d", [TC, C], BF16)
    vk_in = nc.dram_tensor("vk_in", [128, 2112], F32)
    vk_out = nc.dram_tensor("vk_out", [128, 2112], F32)

    def sb(name, shape, dt, **kw):
        return st.enter_context(nc.sbuf_tensor(name, shape, dt, **kw))

    x_sb = sb("x_sb", [128, NS * C], F32)
    tmb = sb("tmb", [128, NS * C], BF16, is_transpose_dest=True)
    fmb = sb("fmb", [128, CCN * TT], BF16, is_transpose_dest=True)
    fuse = sb("fuse", [128, ICN * TT], BF16)
    WA = sb("WA", [128, 8192], BF16)
    WB = sb("WB", [128, 8192], BF16)
    WC = [sb("WC0", [128, 2048], BF16), sb("WC1", [128, 2048], BF16)]
    qstore = sb("qstore", [128, 2 * TT], BF16)
    gs = sb("gs", [128, 2 * TT], BF16)
    us = sb("us", [128, 2 * TT], BF16)
    k_sb = sb("k_sb", [128, 4 * TT], BF16)
    v_sb = sb("v_sb", [128, 4 * 528], BF16)
    vk_acc = sb("vk_acc", [128, 2112], F32)
    vkbd = sb("vkbd", [128, 2112], BF16)
    sg_sb = sb("sg_sb", [128, ICN], F32)
    su_sb = sb("su_sb", [128, ICN], F32)
    bp_sb = sb("bp_sb", [128, CCN], F32)
    sdh_b = sb("sdh_b", [128, C], BF16)
    bp_b = sb("bp_b", [128, C], BF16)
    dtmp = sb("dtmp", [128, 512], F32)
    sx = sb("sx", [128, NS], F32)
    ex2 = sb("ex2", [128, NS], F32)
    lnm = sb("lnm", [128, NS], F32)
    stdv = sb("stdv", [128, NS], F32)
    rstd = sb("rstd", [128, NS], F32)
    rc = sb("rc", [128, 4], F32)
    rc64 = sb("rc64", [128, 64], F32)

    pp = [st.enter_context(nc.psum_tensor(f"pp{i}", [128, 512], F32)) for i in range(8)]

    em = Em(sem_factory=lambda name: st.enter_context(nc.semaphore(name)))
    for sname in ("spe", "sact", "sdve", "spool", "scc"):
        em.sem(sname, st.enter_context(nc.semaphore(sname)))

    Rres = {}
    Rpsum = [None] * 8
    xload = {}

    x_ap = x_h.ap()
    out_ap = out_h.ap()

    sg_r = sg_h.ap().rearrange("(j p one) -> j p one", p=128, one=1)
    su_r = su_h.ap().rearrange("(j p one) -> j p one", p=128, one=1)
    bp_r = bproj_h.ap().rearrange("(j p one) -> j p one", p=128, one=1)
    for j in range(CCN):
        em.dma("gpsimd", "dbp", bp_sb[:, j:j + 1], bp_r[j])
    sdrow = x_sb
    ones1 = vk_acc
    v_sdrow = em.dma("gpsimd", "dbp", sdrow[0:1, 0:C],
                     sd_h.ap().rearrange("(one c) -> one c", one=1))
    v_sdbp = em.cnt["dbp"]
    v_ones1 = em.i("vector", lambda e: e.memset(ones1[0:1, 0:128], 1.0), "sdve")
    em.w("tensor", "sdve", v_ones1)
    em.w("tensor", "dbp", v_sdrow)
    v_sdh_mm = None
    for j in range(4):
        em.wr("tensor", Rpsum[0])
        v_sdh_mm = em.i("tensor", lambda e, j=j: e.matmul(
            pp[0][:], ones1[0:1, 0:128], sdrow[0:1, j * 512:(j + 1) * 512],
            start=True, stop=True), "spe")
        em.w("vector", "spe", v_sdh_mm)
        vev = em.i("vector", lambda e, j=j: e.tensor_scalar(
            out=sdh_b[:, j * 512:(j + 1) * 512], in0=pp[0][:], scalar1=0.5,
            scalar2=None, op0=ALU.mult), "sdve")
        Rpsum[0] = ("sdve", vev)
    em.w("gpsimd", "dp0c", 0)
    v_bprow = em.dma("gpsimd", "dbp", sdrow[0:1, 0:C],
                     bproj_h.ap().rearrange("(one c) -> one c", one=1),
                     [("spe", v_sdh_mm)])
    em.w("tensor", "dbp", v_bprow)
    for j in range(4):
        em.wr("tensor", Rpsum[0])
        v_sdh_mm = em.i("tensor", lambda e, j=j: e.matmul(
            pp[0][:], ones1[0:1, 0:128], sdrow[0:1, j * 512:(j + 1) * 512],
            start=True, stop=True), "spe")
        em.w("vector", "spe", v_sdh_mm)
        vev = em.i("vector", lambda e, j=j: e.tensor_copy(
            bp_b[:, j * 512:(j + 1) * 512], pp[0][:]), "sdve")
        Rpsum[0] = ("sdve", vev)
    em.w("vector", "spe", v_sdh_mm)
    em.i("vector", lambda e: e.memset(vk_acc[:], 0.0), "sdve")
    xload[0] = em.dma("sync", "dx",
                      x_sb[:].rearrange("p (s c) -> p s c", s=NS),
                      x_ap.rearrange("(t s p) c -> t p s c", p=128, s=NS)[0],
                      [("spe", v_sdh_mm)])
    for j in range(ICN):
        em.dma("sync", "dsg", sg_sb[:, j:j + 1], sg_r[j])
        em.dma("scalar", "dsu", su_sb[:, j:j + 1], su_r[j])
    v_sg = em.cnt["dsg"]
    v_su = em.cnt["dsu"]
    vext_ones = v_sb[:].rearrange("p (s h e) -> p s h e", s=4, e=33)[:, :, :, 32:33]
    v_ones = em.i("vector", lambda e, a=vext_ones: e.memset(a, 1.0), "sdve")
    em.i("gpsimd", lambda e: e.memset(vkbd[:], 0.0), "spool")

    ph0 = {"T_done": None}
    NR = (OQKV + C) // 128
    cast_store = {}

    def ph0_cast_job(r):
        slot = r % 2
        if r < OQKV // 128:
            srcw = wqkv_h.ap()[r * 128:(r + 1) * 128, :]
        else:
            srcw = wproj_h.ap()[(r - OQKV // 128) * 128:(r - OQKV // 128 + 1) * 128, :]
        v = em.dma("gpsimd", f"dphc{slot}", WC[slot][:], srcw, [Rres.get(f"WC{slot}")])
        cast_store[r] = (f"dp0c{slot}",
                         em.dma("scalar", f"dp0c{slot}",
                                wqp_bf.ap()[r * 128:(r + 1) * 128, :],
                                WC[slot][:], [(f"dphc{slot}", v)]))
        Rres[f"WC{slot}"] = cast_store[r]

    def ph0_t_job(i):
        if i < 48:
            cc, orng = i % 16, i // 16
            srcp = wqp_bf.ap()[orng * 2048:(orng + 1) * 2048, cc * 128:(cc + 1) * 128]
            dstp = wqkvT.ap()[cc * 128:(cc + 1) * 128, orng * 2048:(orng + 1) * 2048]
            need = [cast_store[(orng + 1) * 16 - 1], cast_store[(orng + 1) * 16 - 2]]
        else:
            cc = i - 48
            srcp = wqp_bf.ap()[OQKV:OQKV + C, cc * 128:(cc + 1) * 128]
            dstp = wprojT.ap()[cc * 128:(cc + 1) * 128, :]
            need = [cast_store[NR - 1], cast_store[NR - 2]]
        v = em.dmat("sync", "dpht", k_sb[:, 0:2048], srcp,
                    need + [Rres.get("k_sb_ph")])
        vs = em.dma("scalar", "dp0t", dstp, k_sb[:, 0:2048], [("dpht", v)])
        Rres["k_sb_ph"] = ("dp0t", vs)
        Rres["k_sb"] = ("dp0t", vs)
        ph0["T_done"] = vs

    ph0_jobs = ([(lambda r=r: ph0_cast_job(r)) for r in range(NR)] +
                [(lambda i=i: ph0_t_job(i)) for i in range(64)])

    def feed_bg(n):
        for _ in range(n):
            if ph0_jobs:
                ph0_jobs.pop(0)()

    def emit_ln_stats(dep_rv, subs):
        for s in subs:
            em.wr("scalar", dep_rv)
            if s == subs[0]:
                em.wr("scalar", Rres.get("tmb"))
                em.wr("scalar", Rres.get("tmb_1"))
            em.i("scalar", lambda e, s=s: e.activation(
                out=tmb[:, s * C:(s + 1) * C], in_=x_sb[:, s * C:(s + 1) * C],
                func=AF.Copy, accum_out=sx[:, s:s + 1]), "sact")
            em.i("scalar", lambda e, s=s: e.activation(
                out=tmb[:, s * C:(s + 1) * C], in_=x_sb[:, s * C:(s + 1) * C],
                func=AF.Square, accum_out=ex2[:, s:s + 1]), "sact")
        return em.cnt["sact"]

    def emit_ln(dep_rv, subs=None):
        if subs is None:
            subs = list(range(NS))
        v_st = emit_ln_stats(dep_rv, subs)
        em.w("vector", "sact", v_st)
        def vflush():
            em.w("vector", "sdve", em.cnt["sdve"])
        em.wr("vector", Rres.get("stdv"))
        vflush()
        em.i("vector", lambda e: e.tensor_scalar(
            out=lnm[:], in0=sx[:], scalar1=1.0 / C, scalar2=None, op0=ALU.mult), "sdve")
        em.i("vector", lambda e: e.tensor_scalar(
            out=ex2[:], in0=ex2[:], scalar1=1.0 / C, scalar2=None, op0=ALU.mult), "sdve")
        vflush()
        em.i("vector", lambda e: e.tensor_tensor(
            out=stdv[:], in0=lnm[:], in1=lnm[:], op=ALU.mult), "sdve")
        vflush()
        em.i("vector", lambda e: e.tensor_tensor(
            out=sx[:], in0=ex2[:], in1=stdv[:], op=ALU.subtract), "sdve")
        vflush()
        v_ve = em.i("vector", lambda e: e.tensor_scalar(
            out=stdv[:], in0=sx[:], scalar1=LN_EPS, scalar2=None, op0=ALU.add), "sdve")
        em.w("scalar", "sdve", v_ve)
        v_sq = em.i("scalar", lambda e: e.activation(
            out=sx[:], in_=stdv[:], func=AF.Sqrt), "sact")
        em.w("vector", "sact", v_sq)
        v_rs = em.i("vector", lambda e: e.reciprocal(rstd[:], sx[:]), "sdve")
        Rres["stdv"] = ("sdve", v_rs)
        vflush()
        outv = None
        for s in range(NS):
            outv = em.i("vector", lambda e, s=s: e.tensor_scalar(
                out=tmb[:, s * C:(s + 1) * C], in0=x_sb[:, s * C:(s + 1) * C],
                scalar1=lnm[:, s:s + 1], scalar2=rstd[:, s:s + 1],
                op0=ALU.subtract, op1=ALU.mult), "sdve")
        Rres["tmb"] = None
        return outv

    def store_tm(tile, dram_h, v_ready):
        dst = dram_h.ap().rearrange("(t s p) c -> t p s c", p=128, s=NS)[tile]
        src = tmb[:].rearrange("p (s c) -> p s c", s=NS)
        v0 = em.dma("scalar", "dty0", dst, src, [("sdve", v_ready)])
        Rres["tmb"] = ("dty0", v0)
        Rres["tmb_1"] = None
        return (v0, 0)

    def tload_fm(tile, dram_h, v_store, extra=()):
        vs = {}
        fmb_guard = Rres.get("fmb")
        for cc in range(CCN):
            ws = ([("dty0", v_store[0])] +
                  ([("dty1", v_store[1])] if v_store[1] else []) + list(extra) +
                  ([fmb_guard] if cc < 1 else []))
            vs["scalar"] = em.dmat("scalar", "dfm0", fmb[:, cc * TT:(cc + 1) * TT],
                                   dram_h.ap()[tile * TT:(tile + 1) * TT,
                                               cc * 128:(cc + 1) * 128], ws)
        return (vs["scalar"], vs["scalar"])

    def tload_tm(tile, dram_h, store_rv):
        vs = None
        for s in range(NS):
            ws = [store_rv] + ([Rres.get("tmb")] if s == 0 else [])
            vs = em.dmat("scalar", "dtm", tmb[:, s * C:(s + 1) * C],
                         dram_h.ap()[:, tile * TT + s * 128: tile * TT + (s + 1) * 128],
                         ws)
        Rres["tmb"] = None
        return vs

    def emit_resid(loads_rv):
        vs = None
        for s in range(NS):
            em.wr("vector", loads_rv)
            vs = em.i("vector", lambda e, s=s: e.tensor_tensor(
                out=x_sb[:, s * C:(s + 1) * C], in0=x_sb[:, s * C:(s + 1) * C],
                in1=tmb[:, s * C:(s + 1) * C], op=ALU.add), "sdve")
        Rres["tmb"] = None
        return vs

    def emit_ffn(tile, fm_ready_dsp, bg=False, mid=None, post_gu=None):
        wg_r = wg_h.ap().rearrange("(cc p) i -> p cc i", p=128)
        wu_r = wu_h.ap().rearrange("(cc p) i -> p cc i", p=128)
        load_g = {}
        load_u = {}
        mm_g = {}
        mm_u = {}
        ev_g = {}
        ev_u = {}
        mul_f = {}
        for ic in range(ICN):
            half = (ic // 2) % 2
            if ic % 2 == 0:
                waits = [Rres.get(f"WAh{half}")]
                if ic < 4:
                    waits += [Rres.get(f"WDq{q}") for q in range(4)] + [Rres.get("WA")]
                dstA = WA[:, half * 4096:(half + 1) * 4096].rearrange(
                    "p (cc i) -> p cc i", cc=CCN)
                load_g[ic] = em.dma("gpsimd", f"dg{half}", dstA,
                                    wg_r[:, :, ic * 128:(ic + 2) * 128], waits)
                waits = [Rres.get(f"WBh{half}")]
                if ic < 4:
                    waits += [Rres.get(f"WDq{q}") for q in range(4)] + [Rres.get("WB")]
                dstB = WB[:, half * 4096:(half + 1) * 4096].rearrange(
                    "p (cc i) -> p cc i", cc=CCN)
                load_u[ic] = em.dma("gpsimd", f"du{half}", dstB,
                                    wu_r[:, :, ic * 128:(ic + 2) * 128], waits)
                if bg:
                    feed_bg(2)
            pg = pp[ic % 2]
            em.w("tensor", f"dg{half}", load_g[ic - ic % 2])
            if ic == 0:
                em.w("tensor", "dfm0", fm_ready_dsp[0])
            em.wr("tensor", Rpsum[ic % 2])
            for cc in range(CCN):
                lhs = WA[:, half * 4096 + cc * 256 + (ic % 2) * 128:
                         half * 4096 + cc * 256 + (ic % 2) * 128 + 128]
                em.i("tensor", lambda e, p=pg, l=lhs, cc=cc: e.matmul(
                    p[:], l, fmb[:, cc * TT:(cc + 1) * TT],
                    start=(cc == 0), stop=(cc == CCN - 1)),
                    "spe" if cc == CCN - 1 else None)
            mm_g[ic] = em.cnt["spe"]
            Rres[f"WAh{half}"] = ("spe", mm_g[ic])
            pu = pp[2 + ic % 2]
            em.w("tensor", f"du{half}", load_u[ic - ic % 2])
            em.wr("tensor", Rpsum[2 + ic % 2])
            for cc in range(CCN):
                lhs = WB[:, half * 4096 + cc * 256 + (ic % 2) * 128:
                         half * 4096 + cc * 256 + (ic % 2) * 128 + 128]
                em.i("tensor", lambda e, p=pu, l=lhs, cc=cc: e.matmul(
                    p[:], l, fmb[:, cc * TT:(cc + 1) * TT],
                    start=(cc == 0), stop=(cc == CCN - 1)),
                    "spe" if cc == CCN - 1 else None)
            mm_u[ic] = em.cnt["spe"]
            Rres[f"WBh{half}"] = ("spe", mm_u[ic])
            slot = ic % 2
            em.w("scalar", "spe", mm_g[ic])
            if ic == 0:
                em.w("scalar", "dsg", v_sg)
                em.w("scalar", "dsu", v_su)
            em.wr("scalar", Rres.get(f"gs{slot}"))
            ev_g[ic] = em.i("scalar", lambda e, p=pg, slot=slot, ic=ic: e.activation(
                out=gs[:, slot * TT:(slot + 1) * TT], in_=p[:], func=AF.Silu,
                scale=sg_sb[:, ic:ic + 1]), "sact")
            em.w("scalar", "spe", mm_u[ic])
            ev_u[ic] = em.i("scalar", lambda e, p=pu, slot=slot, ic=ic: e.activation(
                out=us[:, slot * TT:(slot + 1) * TT], in_=p[:], func=AF.Copy,
                scale=su_sb[:, ic:ic + 1]), "sact")
            Rpsum[ic % 2] = ("sact", ev_g[ic])
            Rpsum[2 + ic % 2] = ("sact", ev_u[ic])
            em.w("vector", "sact", ev_u[ic])
            if ic == 0:
                em.wr("vector", Rres.get("fuse"))
            mul_f[ic] = em.i("vector", lambda e, slot=slot, ic=ic: e.tensor_tensor(
                out=fuse[:, ic * TT:(ic + 1) * TT], in0=gs[:, slot * TT:(slot + 1) * TT],
                in1=us[:, slot * TT:(slot + 1) * TT], op=ALU.mult), "sdve")
            Rres[f"gs{slot}"] = ("sdve", mul_f[ic])
        fuse_all = mul_f[ICN - 1]
        Rres["fmb"] = ("spe", mm_u[ICN - 1])
        if post_gu is not None:
            post_gu()

        v_last_mm = None
        v_last_ev = None
        wd_r2 = wd_h.ap().rearrange("(ic r p) c -> ic p r c", p=128, r=2)
        for pr in range(2):
            mm_ic = {}
            ld_d = {}
            for ic in range(ICN):
                slot = (ic // 2) % 4
                WS = WA if slot < 2 else WB
                ho = slot % 2
                if ic % 2 == 0:
                    waits = [Rres.get(f"WDq{slot}")]
                    if ic - 7 in mm_ic:
                        waits.append(("spe", mm_ic[ic - 7]))
                    if pr == 0 and ic < 8:
                        if slot < 2:
                            waits += [Rres.get(f"WAh{ho}"), Rres.get("WA")]
                        else:
                            waits += [Rres.get(f"WBh{ho}"), Rres.get("WB")]
                    dstd = WS[:, ho * 4096:(ho + 1) * 4096].rearrange(
                        "p (r c) -> p r c", r=2)
                    ld_d[ic] = em.dma("gpsimd", f"dd{slot}", dstd, wd_r2[ic // 2], waits)
                em.w("tensor", f"dd{slot}", ld_d[ic - ic % 2])
                if ic == 0:
                    em.w("tensor", "sdve", fuse_all)
                    for b in range(8):
                        em.wr("tensor", Rpsum[b])
                for s2 in range(2):
                    s = pr * 2 + s2
                    lhs = fuse[:, ic * TT + s * 128: ic * TT + (s + 1) * 128]
                    for ct in range(4):
                        em.i("tensor", lambda e, b=s2 * 4 + ct, l=lhs, W=WS, ho=ho,
                             r=ic % 2, ct=ct, ic=ic: e.matmul(
                                 pp[b][:], l,
                                 W[:, ho * 4096 + r * 2048 + ct * 512:
                                   ho * 4096 + r * 2048 + (ct + 1) * 512],
                                 start=(ic == 0), stop=(ic == ICN - 1)),
                             "spe" if (s2 == 1 and ct == 3) else None)
                mm_ic[ic] = em.cnt["spe"]
            v_last_mm = em.cnt["spe"]
            for slot in range(4):
                Rres[f"WDq{slot}"] = ("spe", v_last_mm)
            em.w("vector", "spe", v_last_mm)
            for s2 in range(2):
                s = pr * 2 + s2
                for ct in range(4):
                    b = s2 * 4 + ct
                    em.w("vector", "sdve", em.cnt["sdve"])
                    em.i("vector", lambda e, b=b, ct=ct: e.tensor_tensor(
                        out=dtmp[:], in0=pp[b][:],
                        in1=sdh_b[:, ct * 512:(ct + 1) * 512], op=ALU.mult), "sdve")
                    em.w("vector", "sdve", em.cnt["sdve"])
                    v_last_ev = em.i("vector", lambda e, s=s, ct=ct: e.tensor_tensor(
                        out=x_sb[:, s * C + ct * 512: s * C + (ct + 1) * 512],
                        in0=x_sb[:, s * C + ct * 512: s * C + (ct + 1) * 512],
                        in1=dtmp[:], op=ALU.add), "sdve")
                    Rpsum[b] = ("sdve", v_last_ev)
            if pr == 0 and mid is not None:
                mid()
        Rres["WA"] = ("spe", v_last_mm)
        Rres["WB"] = ("spe", v_last_mm)
        for hh in range(2):
            Rres[f"WAh{hh}"] = ("spe", v_last_mm)
            Rres[f"WBh{hh}"] = ("spe", v_last_mm)
        Rres["fuse"] = ("spe", v_last_mm)
        return v_last_ev

    xp_store = {}
    x_dve_done = {}

    def emit_pass1_ffnA(t, post_gu=None):
        if t not in xload:
            waits = []
            if t - 1 in xp_store:
                waits.append(("dxp", xp_store[t - 1]))
                waits.append(("sdve", x_dve_done[t - 1]))
            src = x_ap.rearrange("(t s p) c -> t p s c", p=128, s=NS)[t]
            dst = x_sb[:].rearrange("p (s c) -> p s c", s=NS)
            xload[t] = em.dma("scalar", "dx", dst, src, waits)
        vA = emit_ln(("dx", xload[t]))
        svA = store_tm(t, yA_d, vA)
        fm_v = tload_fm(t, yA_d, svA)
        vres = emit_ffn(t, fm_v, bg=(t in (0, 1)),
                        mid=lambda: emit_ln_stats(("sdve", em.cnt["sdve"]), [0, 1]),
                        post_gu=post_gu)
        vB = emit_ln(("sdve", vres), subs=[2, 3])
        svB = store_tm(t, yB_d, vB)
        x_dve_done[t] = vB
        xp_dst = xp_d.ap().rearrange("(t s p) c -> t p s c", p=128, s=NS)[t]
        xp_store[t] = em.dma("sync", "dxp", xp_dst,
                             x_sb[:].rearrange("p (s c) -> p s c", s=NS),
                             [("sdve", vres)])
        if stage == 1:
            o_dst = out_ap.rearrange("(t s p) c -> t p s c", p=128, s=NS)[t]
            xp_store[t] = em.dma("sync", "dxp", o_dst,
                                 x_sb[:].rearrange("p (s c) -> p s c", s=NS),
                                 [("sdve", vres)])
        return svB

    def emit_pass1_qkv(t, svB, fm_v=None):
        if fm_v is None:
            fm_v = tload_fm(t, yB_d, svB)
        wqkvT_r = wqkvT.ap().rearrange("(cc p) o -> p cc o", p=128)
        mm_q = {}
        st_q = {}
        for oc in range(CCN):
            slot = oc % 2
            ld = em.dma("sync", f"dwc{slot}",
                        WC[slot][:].rearrange("p (cc o) -> p cc o", cc=CCN),
                        wqkvT_r[:, :, oc * 128:(oc + 1) * 128],
                        [("dp0t", ph0["T_done"]), Rres.get(f"WC{slot}")])
            pq = pp[oc % 2]
            em.w("tensor", f"dwc{slot}", ld)
            if oc == 0:
                em.w("tensor", "dfm0", fm_v[0])
            em.wr("tensor", Rpsum[oc % 2])
            for cc in range(CCN):
                em.i("tensor", lambda e, p=pq, W=WC[slot], cc=cc: e.matmul(
                    p[:], W[:, cc * 128:(cc + 1) * 128], fmb[:, cc * TT:(cc + 1) * TT],
                    start=(cc == 0), stop=(cc == CCN - 1)),
                    "spe" if cc == CCN - 1 else None)
            mm_q[oc] = em.cnt["spe"]
            Rres[f"WC{slot}"] = ("spe", mm_q[oc])
            em.w("scalar", "spe", mm_q[oc])
            em.wr("scalar", Rres.get(f"qstore{slot}"))
            ev = em.i("scalar", lambda e, p=pq, slot=slot: e.activation(
                out=qstore[:, slot * TT:(slot + 1) * TT], in_=p[:], func=AF.Relu), "sact")
            Rpsum[oc % 2] = ("sact", ev)
            st_q[oc] = em.dma("scalar", f"dqs{slot}",
                              q_d.ap()[oc * 128:(oc + 1) * 128, t * TT:(t + 1) * TT],
                              qstore[:, slot * TT:(slot + 1) * TT],
                              [("sact", ev)])
            Rres[f"qstore{slot}"] = (f"dqs{slot}", st_q[oc])
        vk_ev_last = None
        for j in range(4):
            ldk = em.dma("sync", "dk", WA[:].rearrange("p (cc o) -> p cc o", cc=CCN),
                         wqkvT_r[:, :, C + j * TT:C + (j + 1) * TT],
                         [("dp0t", ph0["T_done"]), Rres.get("WAh0"), Rres.get("WAh1")] +
                         [Rres.get(f"WDq{q}") for q in range(4)] + [Rres.get("WA")])
            ldv = em.dma("sync", "dv", WB[:].rearrange("p (cc o) -> p cc o", cc=CCN),
                         wqkvT_r[:, :, 2 * C + j * TT:2 * C + (j + 1) * TT],
                         [("dp0t", ph0["T_done"]), Rres.get("WBh0"), Rres.get("WBh1")] +
                         [Rres.get(f"WDq{q}") for q in range(4)] + [Rres.get("WB")])
            mm_k = {}
            mm_v = {}
            ev_k = {}
            ev_v = {}
            for s in range(NS):
                pk = pp[2 + s % 2]
                em.w("tensor", "dk", ldk)
                if (j, s) == (0, 0):
                    em.w("tensor", "dfm0", fm_v[0])
                em.wr("tensor", Rpsum[2 + s % 2])
                for cc in range(CCN):
                    em.i("tensor", lambda e, p=pk, cc=cc, s=s: e.matmul(
                        p[:], fmb[:, cc * TT + s * 128: cc * TT + (s + 1) * 128],
                        WA[:, cc * TT:(cc + 1) * TT],
                        start=(cc == 0), stop=(cc == CCN - 1)),
                        "spe" if cc == CCN - 1 else None)
                mm_k[s] = em.cnt["spe"]
                em.w("scalar", "spe", mm_k[s])
                em.wr("scalar", Rres.get("k_sb"))
                ev_k[s] = em.i("scalar", lambda e, p=pk, s=s: e.activation(
                    out=k_sb[:, s * TT:(s + 1) * TT], in_=p[:], func=AF.Relu), "sact")
                Rpsum[2 + s % 2] = ("sact", ev_k[s])
                pv = pp[4 + s % 2]
                em.w("tensor", "dv", ldv)
                em.wr("tensor", Rpsum[4 + s % 2])
                for cc in range(CCN):
                    em.i("tensor", lambda e, p=pv, cc=cc, s=s: e.matmul(
                        p[:], fmb[:, cc * TT + s * 128: cc * TT + (s + 1) * 128],
                        WB[:, cc * TT:(cc + 1) * TT],
                        start=(cc == 0), stop=(cc == CCN - 1)),
                        "spe" if cc == CCN - 1 else None)
                mm_v[s] = em.cnt["spe"]
                em.w("vector", "spe", mm_v[s])
                em.wr("vector", Rres.get("v_sb"))
                if (j, s) == (0, 0):
                    em.w("vector", "sdve", v_ones)
                vdst = v_sb[:, s * 528:(s + 1) * 528].rearrange(
                    "p (h e) -> p h e", h=16)[:, :, 0:32]
                vsrc = pv[:].rearrange("p (h e) -> p h e", e=32)
                ev_v[s] = em.i("vector", lambda e, d=vdst, sr=vsrc: e.tensor_copy(d, sr),
                               "sdve")
                Rpsum[4 + s % 2] = ("sdve", ev_v[s])
            Rres["WA"] = ("spe", mm_k[NS - 1])
            Rres["WB"] = ("spe", mm_v[NS - 1])
            for ql in range(4):
                qg = 4 * j + ql
                b = 6 + ql % 2
                pvx = pp[b]
                em.w("tensor", "sact", ev_k[NS - 1])
                em.w("tensor", "sdve", ev_v[NS - 1])
                em.wr("tensor", Rpsum[b])
                for s in range(NS):
                    em.i("tensor", lambda e, p=pvx, s=s, ql=ql: e.matmul(
                        p[:, 0:132], k_sb[:, s * TT + ql * 128: s * TT + (ql + 1) * 128],
                        v_sb[:, s * 528 + ql * 132: s * 528 + (ql + 1) * 132],
                        start=(s == 0), stop=(s == NS - 1)),
                        "spe" if s == NS - 1 else None)
                v_mm = em.cnt["spe"]
                em.w("vector", "spe", v_mm)
                vk_ev_last = em.i("vector", lambda e, p=pvx, qg=qg: e.tensor_tensor(
                    out=vk_acc[:, qg * 132:(qg + 1) * 132],
                    in0=vk_acc[:, qg * 132:(qg + 1) * 132],
                    in1=p[:, 0:132], op=ALU.add), "sdve")
                Rpsum[b] = ("sdve", vk_ev_last)
                Rres["k_sb"] = ("spe", v_mm)
                Rres["v_sb"] = ("spe", v_mm)
        Rres["fmb"] = ("spe", em.cnt["spe"])
        return vk_ev_last

    def emit_collective(v_vk_last):
        v_st = em.dma("scalar", "dvk", vk_in.ap(), vk_acc[:], [("sdve", v_vk_last)])
        em.w("gpsimd", "dvk", v_st)
        em.i("gpsimd", lambda e: e.collective_compute(
            "AllReduce", ALU.add,
            replica_groups=[[0, 1], [2, 3], [4, 5], [6, 7]],
            ins=[vk_in.ap().opt()], outs=[vk_out.ap().opt()]), "scc", 1)
        em.w("gpsimd", "scc", em.cnt["scc"])
        vs = None
        for a in range(4):
            src = vk_out.ap()[32 * a:32 * (a + 1), :].rearrange(
                "p (q e) -> p q e", q=16)[:, :, 33 * a:33 * a + 33]
            dst = vkbd[32 * a:32 * (a + 1), :].rearrange(
                "p (q e) -> p q e", q=16)[:, :, 33 * a:33 * a + 33]
            vs = em.dma("gpsimd", "dvkbd", dst, src)
        return vs

    out_store = {}
    sv_att3 = {}
    svC3 = {}

    def emit_p3_attn(t, v_vkbd):
        qsrc = q_d.ap().rearrange("(oc p) tt -> p oc tt", p=128)[:, :, t * TT:(t + 1) * TT]
        qdst = fmb[:].rearrange("p (oc tt) -> p oc tt", oc=CCN)
        vq = em.dma("scalar", "dfmq", qdst, qsrc,
                    [("dqs0", st_q_last[t][0]), ("dqs1", st_q_last[t][1]),
                     Rres.get("fmb")])
        att_raw = v_sb
        att_mm_last = None
        sv_att = None
        sv_atts = {}
        for s in range(NS):
            cps = {}
            for oc in range(CCN):
                idx = s * CCN + oc
                b = idx % 6
                pvx = pp[b]
                em.w("tensor", "dfmq", vq)
                if t == 0 and idx == 0:
                    em.w("tensor", "dvkbd", v_vkbd)
                em.wr("tensor", Rpsum[b])
                em.i("tensor", lambda e, p=pvx, s=s, oc=oc: e.matmul(
                    p[:, 0:132], fmb[:, oc * TT + s * 128: oc * TT + (s + 1) * 128],
                    vkbd[:, oc * 132:(oc + 1) * 132], start=True, stop=True), "spe")
                att_mm_last = em.cnt["spe"]
                em.w("scalar", "spe", att_mm_last)
                if s == 0 and oc == 0:
                    em.wr("scalar", Rres.get("v_sb"))
                cps[oc] = em.i("scalar", lambda e, p=pvx, oc=oc: e.activation(
                    out=att_raw[:, oc * 132:(oc + 1) * 132], in_=p[:, 0:132],
                    func=AF.Copy), "sact")
                Rpsum[b] = ("sact", cps[oc])
            em.w("vector", "sact", cps[CCN - 1])
            if s == 0:
                em.wr("vector", Rres.get("tmb"))
                em.wr("vector", Rres.get("tmb_1"))
            em.w("vector", "sdve", em.cnt["sdve"])
            em.i("vector", lambda e: e.tensor_scalar(
                out=rc64[:], in0=att_raw[:, 32::33], scalar1=ATT_EPS, scalar2=None,
                op0=ALU.add), "sdve")
            em.w("vector", "sdve", em.cnt["sdve"])
            em.i("vector", lambda e: e.reciprocal(rc64[:], rc64[:]), "sdve")
            em.w("vector", "sdve", em.cnt["sdve"])
            num = att_raw[:].rearrange("p (g e) -> p g e", g=CCN * 4)[:, :, 0:32]
            den = rc64[:].rearrange("p (g one) -> p g one", one=1)
            num_b, den_b = bass.broadcast_tensor_aps(num, den)
            dstv = tmb[:, s * C:(s + 1) * C].rearrange("p (g e) -> p g e", g=CCN * 4)
            div_v = em.i("vector", lambda e, d=dstv, n=num_b, r=den_b: e.tensor_tensor(
                out=d, in0=n, in1=r, op=ALU.mult), "sdve")
            Rres["v_sb"] = ("sdve", div_v)
            dst = att_d.ap().rearrange("(t s p) c -> t s p c", p=128, s=NS)[t][s]
            sv_att = em.dma("scalar", "dty0", dst, tmb[:, s * C:(s + 1) * C],
                            [("sdve", div_v)])
            sv_atts[0] = sv_att
            sv_atts[1] = 0
        Rres["tmb"] = ("dty0", sv_atts[0])
        Rres["tmb_1"] = None
        Rres["fmb"] = ("spe", att_mm_last)
        sv_att3[t] = (sv_atts[0], sv_atts[1])

    def emit_p3_projln(t):
        if t == 0:
            waits = [("dxp", xp_store[NT - 1]), ("sdve", x_dve_done[NT - 1])]
        else:
            waits = [("dxp", max(xp_store[t], out_store[t - 1]))]
        srcx = xp_d.ap().rearrange("(t s p) c -> t p s c", p=128, s=NS)[t]
        dstx = x_sb[:].rearrange("p (s c) -> p s c", s=NS)
        vx = em.dma("scalar", "dx", dstx, srcx, waits)
        fm_v = tload_fm(t, att_d, sv_att3[t])
        wprojT_r2 = wprojT.ap().rearrange("(oc p) o2 -> oc p o2", p=128)
        v_last_mm = None
        vres2 = None
        for pr in range(2):
            mm_oc = {}
            ld_p = {}
            for oc in range(CCN):
                slot = oc % 2
                waits = [Rres.get(f"WPq{slot}")]
                if oc - 2 in mm_oc:
                    waits.append(("spe", mm_oc[oc - 2]))
                if pr == 0 and oc < 2:
                    waits.append(("dp0t", ph0["T_done"]))
                    waits.append(Rres.get(f"WC{slot}"))
                ld_p[oc] = em.dma("sync", f"dwc{slot}", WC[slot][:], wprojT_r2[oc], waits)
                em.w("tensor", f"dwc{slot}", ld_p[oc])
                if oc == 0:
                    em.w("tensor", "dfm0", fm_v[0])
                    em.w("vector", "dx", vx)
                    for b in range(8):
                        em.wr("tensor", Rpsum[b])
                for s2 in range(2):
                    s = pr * 2 + s2
                    lhs = fmb[:, oc * TT + s * 128: oc * TT + (s + 1) * 128]
                    for ct in range(4):
                        em.i("tensor", lambda e, b=s2 * 4 + ct, l=lhs, slot=slot, ct=ct,
                             oc=oc: e.matmul(
                                 pp[b][:], l,
                                 WC[slot][:, ct * 512:(ct + 1) * 512],
                                 start=(oc == 0), stop=(oc == CCN - 1)),
                             "spe" if (s2 == 1 and ct == 3) else None)
                mm_oc[oc] = em.cnt["spe"]
            v_last_mm = em.cnt["spe"]
            for slot in range(2):
                Rres[f"WPq{slot}"] = ("spe", v_last_mm)
            em.w("vector", "spe", v_last_mm)
            for s2 in range(2):
                s = pr * 2 + s2
                for ct in range(4):
                    b = s2 * 4 + ct
                    em.w("vector", "sdve", em.cnt["sdve"])
                    em.i("vector", lambda e, b=b, ct=ct: e.tensor_tensor(
                        out=dtmp[:], in0=pp[b][:],
                        in1=bp_b[:, ct * 512:(ct + 1) * 512], op=ALU.add), "sdve")
                    em.w("vector", "sdve", em.cnt["sdve"])
                    vres2 = em.i("vector", lambda e, s=s, ct=ct: e.tensor_tensor(
                        out=x_sb[:, s * C + ct * 512: s * C + (ct + 1) * 512],
                        in0=x_sb[:, s * C + ct * 512: s * C + (ct + 1) * 512],
                        in1=dtmp[:], op=ALU.add), "sdve")
                    Rpsum[b] = ("sdve", vres2)
            if pr == 0:
                emit_ln_stats(("sdve", em.cnt["sdve"]), [0, 1])
        Rres["WC0"] = ("spe", v_last_mm)
        Rres["WC1"] = ("spe", v_last_mm)
        Rres["fmb"] = ("spe", v_last_mm)
        if stage == 2:
            o_dst = out_ap.rearrange("(t s p) c -> t p s c", p=128, s=NS)[t]
            out_store[t] = em.dma("sync", "dxp", o_dst,
                                  x_sb[:].rearrange("p (s c) -> p s c", s=NS),
                                  [("sdve", vres2)])
            return
        vC = emit_ln(("sdve", vres2), subs=[2, 3])
        svC3[t] = store_tm(t, yC_d, vC)

    def emit_p3_ffnC(t):
        fm_v2 = tload_fm(t, yC_d, svC3[t])
        vres3 = emit_ffn(t, fm_v2)
        o_dst = out_ap.rearrange("(t s p) c -> t p s c", p=128, s=NS)[t]
        out_store[t] = em.dma("sync", "dxp", o_dst,
                              x_sb[:].rearrange("p (s c) -> p s c", s=NS),
                              [("sdve", vres3)])

    st_q_last = {}
    vk_last = None
    svB = {}
    fmq = {}

    def _mk_post_gu(tq):
        def h():
            fmq[tq] = tload_fm(tq, yB_d, svB[tq])
        return h

    svB[0] = emit_pass1_ffnA(0)
    svB[1] = emit_pass1_ffnA(1, post_gu=_mk_post_gu(0))
    feed_bg(len(ph0_jobs))
    for t in range(NT):
        vk_last = emit_pass1_qkv(t, svB[t], fm_v=fmq.get(t))
        st_q_last[t] = (em.cnt["dqs0"], em.cnt["dqs1"])
        if t + 2 < NT:
            svB[t + 2] = emit_pass1_ffnA(t + 2, post_gu=_mk_post_gu(t + 1))
    if stage >= 2 and stage != 11:
        v_vkbd = emit_collective(vk_last)
        if stage == 2:
            for t in range(NT):
                emit_p3_attn(t, v_vkbd)
                emit_p3_projln(t)
        else:
            emit_p3_attn(0, v_vkbd)
            emit_p3_attn(1, v_vkbd)
            emit_p3_projln(0)
            emit_p3_attn(2, v_vkbd)
            emit_p3_ffnC(0)
            emit_p3_projln(1)
            emit_p3_attn(3, v_vkbd)
            emit_p3_ffnC(1)
            emit_p3_projln(2)
            emit_p3_ffnC(2)
            emit_p3_projln(3)
            emit_p3_ffnC(3)
        final_v = out_store[NT - 1]
    else:
        final_v = xp_store[NT - 1]
    em.w("sync", "dxp", final_v)

    with nc.Block() as block:
        @block.tensor
        def _(e):
            for f in em.streams["tensor"]:
                f(e)

        @block.scalar
        def _(e):
            for f in em.streams["scalar"]:
                f(e)

        @block.vector
        def _(e):
            for f in em.streams["vector"]:
                f(e)

        @block.gpsimd
        def _(e):
            for f in em.streams["gpsimd"]:
                f(e)

        @block.sync
        def _(e):
            for f in em.streams["sync"]:
                f(e)

    st.close()
    return nc


_NC_CACHE = {}


def kernel(L0_attn_x, L0_attn_w_qkv, L0_attn_w_proj, L0_attn_b_proj,
           L0_ffn_w_gate, L0_ffn_w_up, L0_ffn_w_down,
           L0_ffn_s_gate, L0_ffn_s_up, L0_ffn_s_down, _stage=3):
    x = np.ascontiguousarray(np.asarray(L0_attn_x, dtype=np.float32)).reshape(NCORES * TC, C)
    wqkv = np.ascontiguousarray(np.asarray(L0_attn_w_qkv, dtype=np.float32))
    wproj = np.ascontiguousarray(np.asarray(L0_attn_w_proj, dtype=np.float32))
    bproj = np.ascontiguousarray(np.asarray(L0_attn_b_proj, dtype=np.float32))
    wg = np.ascontiguousarray(np.asarray(L0_ffn_w_gate, dtype=np.int8))
    wu = np.ascontiguousarray(np.asarray(L0_ffn_w_up, dtype=np.int8))
    wd = np.ascontiguousarray(np.asarray(L0_ffn_w_down, dtype=np.int8))
    sg = np.ascontiguousarray(np.asarray(L0_ffn_s_gate, dtype=np.float32))
    su = np.ascontiguousarray(np.asarray(L0_ffn_s_up, dtype=np.float32))
    sd = np.ascontiguousarray(np.asarray(L0_ffn_s_down, dtype=np.float32))

    if _stage not in _NC_CACHE:
        _NC_CACHE[_stage] = build(stage=_stage)
    nc = _NC_CACHE[_stage]

    in_maps = []
    for c in range(NCORES):
        in_maps.append({
            "x": x[c * TC:(c + 1) * TC],
            "wqkv": wqkv, "wproj": wproj, "bproj": bproj,
            "wg": wg, "wu": wu, "wd": wd,
            "sg": sg, "su": su, "sd": sd,
        })
    res = run_bass_kernel_spmd(nc, in_maps, core_ids=list(range(NCORES)))
    if _stage == 11:
        return res.results
    out = np.concatenate([np.asarray(res.results[c]["out"]) for c in range(NCORES)], axis=0)
    return out.reshape(4, 4096, 2048)


# version 20
# speedup vs baseline: 1.1190x; 1.1190x over previous
import numpy as np
import contextlib
import concourse.bass as bass
import concourse.mybir as mybir
import concourse.bass_utils as _bass_utils
from concourse.bass_utils import run_bass_kernel_spmd


F32 = mybir.dt.float32
BF16 = mybir.dt.bfloat16
I8 = mybir.dt.int8
AF = mybir.ActivationFunctionType
ALU = mybir.AluOpType

NCORES = 8
TC = 2048
NT = 4
TT = 512
NS = 4
C = 2048
CCN = 16
INTER = 8192
ICN = 64
OQKV = 6144
LN_EPS = 1e-5
ATT_EPS = 1e-15

ENGS = ("tensor", "scalar", "vector", "gpsimd", "sync")
QSEM = {"sync": "dsp", "scalar": "dsc", "gpsimd": "dgp"}


class Em:
    def __init__(self, sem_factory=None):
        self.streams = {k: [] for k in ENGS}
        self.cnt = {}
        self.hdl = {}
        self.sem_factory = sem_factory

    def sem(self, name, h):
        self.hdl[name] = h
        self.cnt[name] = 0

    def ensure(self, name):
        if name not in self.hdl:
            self.sem(name, self.sem_factory(name))

    def w(self, eng, sname, val):
        if val is None or val <= 0:
            return
        assert val <= self.cnt[sname], (eng, sname, val, self.cnt[sname])
        h = self.hdl[sname]
        self.streams[eng].append(lambda e, h=h, v=val: e.wait_ge(h, v))

    def wr(self, eng, rv):
        if rv is not None:
            self.w(eng, rv[0], rv[1])

    def i(self, eng, f, sname=None, n=1):
        if sname is None:
            self.streams[eng].append(f)
            return None
        h = self.hdl[sname]
        self.cnt[sname] += n
        self.streams[eng].append(lambda e, f=f, h=h, n=n: f(e).then_inc(h, n))
        return self.cnt[sname]

    def dma(self, q, sname, out_ap, in_ap, waits=()):
        self.ensure(sname)
        for rv in waits:
            self.wr(q, rv)
        return self.i(q, lambda e, o=out_ap, i=in_ap: e.dma_start(out=o, in_=i),
                      sname, 16)

    def dmat(self, q, sname, out_ap, in_ap, waits=()):
        self.ensure(sname)
        for rv in waits:
            self.wr(q, rv)
        return self.i(q, lambda e, o=out_ap, i=in_ap: e.dma_start_transpose(o, i),
                      sname, 16)


def build(stage=3):
    nc = bass.Bass("TRN2", target_bir_lowering=False)
    st = contextlib.ExitStack()

    x_h = nc.dram_tensor("x", [TC, C], F32, kind="ExternalInput")
    wqkv_h = nc.dram_tensor("wqkv", [OQKV, C], F32, kind="ExternalInput")
    wproj_h = nc.dram_tensor("wproj", [C, C], F32, kind="ExternalInput")
    bproj_h = nc.dram_tensor("bproj", [C], F32, kind="ExternalInput")
    wg_h = nc.dram_tensor("wg", [C, INTER], I8, kind="ExternalInput")
    wu_h = nc.dram_tensor("wu", [C, INTER], I8, kind="ExternalInput")
    wd_h = nc.dram_tensor("wd", [INTER, C], I8, kind="ExternalInput")
    sg_h = nc.dram_tensor("sg", [INTER], F32, kind="ExternalInput")
    su_h = nc.dram_tensor("su", [INTER], F32, kind="ExternalInput")
    sd_h = nc.dram_tensor("sd", [C], F32, kind="ExternalInput")
    out_h = nc.dram_tensor("out", [TC, C], F32, kind="ExternalOutput")

    wqkvT = nc.dram_tensor("wqkvT", [C, OQKV], BF16)
    wprojT = nc.dram_tensor("wprojT", [C, C], BF16)
    wqp_bf = nc.dram_tensor("wqp_bf", [OQKV + C, C], BF16)
    xp_d = nc.dram_tensor("xp_d", [TC, C], F32)
    q_d = nc.dram_tensor("q_d", [C, TC], BF16)
    yA_d = nc.dram_tensor("yA_d", [TC, C], BF16)
    yB_d = nc.dram_tensor("yB_d", [TC, C], BF16)
    yC_d = nc.dram_tensor("yC_d", [TC, C], BF16)
    dP_d = nc.dram_tensor("dP_d", [C, TC], BF16)
    att_d = nc.dram_tensor("att_d", [TC, C], BF16)
    vk_in = nc.dram_tensor("vk_in", [128, 2112], F32)
    vk_out = nc.dram_tensor("vk_out", [128, 2112], F32)

    def sb(name, shape, dt, **kw):
        return st.enter_context(nc.sbuf_tensor(name, shape, dt, **kw))

    x_sb = sb("x_sb", [128, NS * C], F32)
    tmb = sb("tmb", [128, NS * C], BF16, is_transpose_dest=True)
    fmb = sb("fmb", [128, CCN * TT], BF16, is_transpose_dest=True)
    fuse = sb("fuse", [128, ICN * TT], BF16)
    WA = sb("WA", [128, 8192], BF16)
    WB = sb("WB", [128, 8192], BF16)
    WC = [sb("WC0", [128, 2048], BF16), sb("WC1", [128, 2048], BF16)]
    qstore = sb("qstore", [128, 2 * TT], BF16)
    gs = sb("gs", [128, 2 * TT], BF16)
    us = sb("us", [128, 2 * TT], BF16)
    k_sb = sb("k_sb", [128, 4 * TT], BF16)
    v_sb = sb("v_sb", [128, 4 * 528], BF16)
    vk_acc = sb("vk_acc", [128, 2112], F32)
    vkbd = sb("vkbd", [128, 2112], BF16)
    sg_sb = sb("sg_sb", [128, ICN], F32)
    su_sb = sb("su_sb", [128, ICN], F32)
    bp_sb = sb("bp_sb", [128, CCN], F32)
    sdh_b = sb("sdh_b", [128, C], BF16)
    bp_b = sb("bp_b", [128, C], BF16)
    dtmp = sb("dtmp", [128, 512], F32)
    sx = sb("sx", [128, NS], F32)
    ex2 = sb("ex2", [128, NS], F32)
    lnm = sb("lnm", [128, NS], F32)
    stdv = sb("stdv", [128, NS], F32)
    rstd = sb("rstd", [128, NS], F32)
    rc = sb("rc", [128, 4], F32)
    rc64 = sb("rc64", [128, 64], F32)

    pp = [st.enter_context(nc.psum_tensor(f"pp{i}", [128, 512], F32)) for i in range(8)]

    em = Em(sem_factory=lambda name: st.enter_context(nc.semaphore(name)))
    for sname in ("spe", "sact", "sdve", "spool", "scc"):
        em.sem(sname, st.enter_context(nc.semaphore(sname)))

    Rres = {}
    Rpsum = [None] * 8
    xload = {}

    x_ap = x_h.ap()
    out_ap = out_h.ap()

    sg_r = sg_h.ap().rearrange("(j p one) -> j p one", p=128, one=1)
    su_r = su_h.ap().rearrange("(j p one) -> j p one", p=128, one=1)
    bp_r = bproj_h.ap().rearrange("(j p one) -> j p one", p=128, one=1)
    for j in range(CCN):
        em.dma("gpsimd", "dbp", bp_sb[:, j:j + 1], bp_r[j])
    sdrow = x_sb
    ones1 = vk_acc
    v_sdrow = em.dma("gpsimd", "dbp", sdrow[0:1, 0:C],
                     sd_h.ap().rearrange("(one c) -> one c", one=1))
    v_sdbp = em.cnt["dbp"]
    v_ones1 = em.i("vector", lambda e: e.memset(ones1[0:1, 0:128], 1.0), "sdve")
    em.w("tensor", "sdve", v_ones1)
    em.w("tensor", "dbp", v_sdrow)
    v_sdh_mm = None
    for j in range(4):
        em.wr("tensor", Rpsum[0])
        v_sdh_mm = em.i("tensor", lambda e, j=j: e.matmul(
            pp[0][:], ones1[0:1, 0:128], sdrow[0:1, j * 512:(j + 1) * 512],
            start=True, stop=True), "spe")
        em.w("vector", "spe", v_sdh_mm)
        vev = em.i("vector", lambda e, j=j: e.tensor_scalar(
            out=sdh_b[:, j * 512:(j + 1) * 512], in0=pp[0][:], scalar1=0.5,
            scalar2=None, op0=ALU.mult), "sdve")
        Rpsum[0] = ("sdve", vev)
    em.w("gpsimd", "dp0c", 0)
    v_bprow = em.dma("gpsimd", "dbp", sdrow[0:1, 0:C],
                     bproj_h.ap().rearrange("(one c) -> one c", one=1),
                     [("spe", v_sdh_mm)])
    em.w("tensor", "dbp", v_bprow)
    for j in range(4):
        em.wr("tensor", Rpsum[0])
        v_sdh_mm = em.i("tensor", lambda e, j=j: e.matmul(
            pp[0][:], ones1[0:1, 0:128], sdrow[0:1, j * 512:(j + 1) * 512],
            start=True, stop=True), "spe")
        em.w("vector", "spe", v_sdh_mm)
        vev = em.i("vector", lambda e, j=j: e.tensor_copy(
            bp_b[:, j * 512:(j + 1) * 512], pp[0][:]), "sdve")
        Rpsum[0] = ("sdve", vev)
    em.w("vector", "spe", v_sdh_mm)
    em.i("vector", lambda e: e.memset(vk_acc[:], 0.0), "sdve")
    xload[0] = em.dma("sync", "dx",
                      x_sb[:].rearrange("p (s c) -> p s c", s=NS),
                      x_ap.rearrange("(t s p) c -> t p s c", p=128, s=NS)[0],
                      [("spe", v_sdh_mm)])
    for j in range(ICN):
        em.dma("sync", "dsg", sg_sb[:, j:j + 1], sg_r[j])
        em.dma("scalar", "dsu", su_sb[:, j:j + 1], su_r[j])
    v_sg = em.cnt["dsg"]
    v_su = em.cnt["dsu"]
    vext_ones = v_sb[:].rearrange("p (s h e) -> p s h e", s=4, e=33)[:, :, :, 32:33]
    v_ones = em.i("vector", lambda e, a=vext_ones: e.memset(a, 1.0), "sdve")
    em.i("gpsimd", lambda e: e.memset(vkbd[:], 0.0), "spool")

    ph0 = {"T_done": None}
    NR = (OQKV + C) // 128
    cast_store = {}

    def ph0_cast_job(r):
        slot = r % 2
        if r < OQKV // 128:
            srcw = wqkv_h.ap()[r * 128:(r + 1) * 128, :]
        else:
            srcw = wproj_h.ap()[(r - OQKV // 128) * 128:(r - OQKV // 128 + 1) * 128, :]
        v = em.dma("gpsimd", f"dphc{slot}", WC[slot][:], srcw, [Rres.get(f"WC{slot}")])
        cast_store[r] = (f"dp0c{slot}",
                         em.dma("scalar", f"dp0c{slot}",
                                wqp_bf.ap()[r * 128:(r + 1) * 128, :],
                                WC[slot][:], [(f"dphc{slot}", v)]))
        Rres[f"WC{slot}"] = cast_store[r]

    def ph0_t_job(i):
        if i < 48:
            cc, orng = i % 16, i // 16
            srcp = wqp_bf.ap()[orng * 2048:(orng + 1) * 2048, cc * 128:(cc + 1) * 128]
            dstp = wqkvT.ap()[cc * 128:(cc + 1) * 128, orng * 2048:(orng + 1) * 2048]
            need = [cast_store[(orng + 1) * 16 - 1], cast_store[(orng + 1) * 16 - 2]]
        else:
            cc = i - 48
            srcp = wqp_bf.ap()[OQKV:OQKV + C, cc * 128:(cc + 1) * 128]
            dstp = wprojT.ap()[cc * 128:(cc + 1) * 128, :]
            need = [cast_store[NR - 1], cast_store[NR - 2]]
        v = em.dmat("sync", "dpht", k_sb[:, 0:2048], srcp,
                    need + [Rres.get("k_sb_ph")])
        vs = em.dma("scalar", "dp0t", dstp, k_sb[:, 0:2048], [("dpht", v)])
        Rres["k_sb_ph"] = ("dp0t", vs)
        Rres["k_sb"] = ("dp0t", vs)
        ph0["T_done"] = vs

    ph0_jobs = ([(lambda r=r: ph0_cast_job(r)) for r in range(NR)] +
                [(lambda i=i: ph0_t_job(i)) for i in range(64)])

    def feed_bg(n):
        for _ in range(n):
            if ph0_jobs:
                ph0_jobs.pop(0)()

    def emit_ln_stats(dep_rv, subs):
        for s in subs:
            em.wr("scalar", dep_rv)
            if s == subs[0]:
                em.wr("scalar", Rres.get("tmb"))
                em.wr("scalar", Rres.get("tmb_1"))
            em.i("scalar", lambda e, s=s: e.activation(
                out=tmb[:, s * C:(s + 1) * C], in_=x_sb[:, s * C:(s + 1) * C],
                func=AF.Copy, accum_out=sx[:, s:s + 1]), "sact")
            em.i("scalar", lambda e, s=s: e.activation(
                out=tmb[:, s * C:(s + 1) * C], in_=x_sb[:, s * C:(s + 1) * C],
                func=AF.Square, accum_out=ex2[:, s:s + 1]), "sact")
        return em.cnt["sact"]

    def emit_ln(dep_rv, subs=None):
        if subs is None:
            subs = list(range(NS))
        v_st = emit_ln_stats(dep_rv, subs)
        em.w("vector", "sact", v_st)
        def vflush():
            em.w("vector", "sdve", em.cnt["sdve"])
        em.wr("vector", Rres.get("stdv"))
        vflush()
        em.i("vector", lambda e: e.tensor_scalar(
            out=lnm[:], in0=sx[:], scalar1=1.0 / C, scalar2=None, op0=ALU.mult), "sdve")
        em.i("vector", lambda e: e.tensor_scalar(
            out=ex2[:], in0=ex2[:], scalar1=1.0 / C, scalar2=None, op0=ALU.mult), "sdve")
        vflush()
        em.i("vector", lambda e: e.tensor_tensor(
            out=stdv[:], in0=lnm[:], in1=lnm[:], op=ALU.mult), "sdve")
        vflush()
        em.i("vector", lambda e: e.tensor_tensor(
            out=sx[:], in0=ex2[:], in1=stdv[:], op=ALU.subtract), "sdve")
        vflush()
        v_ve = em.i("vector", lambda e: e.tensor_scalar(
            out=stdv[:], in0=sx[:], scalar1=LN_EPS, scalar2=None, op0=ALU.add), "sdve")
        em.w("scalar", "sdve", v_ve)
        v_sq = em.i("scalar", lambda e: e.activation(
            out=sx[:], in_=stdv[:], func=AF.Sqrt), "sact")
        em.w("vector", "sact", v_sq)
        v_rs = em.i("vector", lambda e: e.reciprocal(rstd[:], sx[:]), "sdve")
        Rres["stdv"] = ("sdve", v_rs)
        vflush()
        outv = None
        for s in range(NS):
            outv = em.i("vector", lambda e, s=s: e.tensor_scalar(
                out=tmb[:, s * C:(s + 1) * C], in0=x_sb[:, s * C:(s + 1) * C],
                scalar1=lnm[:, s:s + 1], scalar2=rstd[:, s:s + 1],
                op0=ALU.subtract, op1=ALU.mult), "sdve")
        Rres["tmb"] = None
        return outv

    def store_tm(tile, dram_h, v_ready):
        dst = dram_h.ap().rearrange("(t s p) c -> t p s c", p=128, s=NS)[tile]
        src = tmb[:].rearrange("p (s c) -> p s c", s=NS)
        v0 = em.dma("scalar", "dty0", dst, src, [("sdve", v_ready)])
        Rres["tmb"] = ("dty0", v0)
        Rres["tmb_1"] = None
        return (v0, 0)

    def tload_fm(tile, dram_h, v_store, extra=()):
        vs = {}
        fmb_guard = Rres.get("fmb")
        for cc in range(CCN):
            ws = ([("dty0", v_store[0])] +
                  ([("dty1", v_store[1])] if v_store[1] else []) + list(extra) +
                  ([fmb_guard] if cc < 1 else []))
            vs["scalar"] = em.dmat("scalar", "dfm0", fmb[:, cc * TT:(cc + 1) * TT],
                                   dram_h.ap()[tile * TT:(tile + 1) * TT,
                                               cc * 128:(cc + 1) * 128], ws)
        return (vs["scalar"], vs["scalar"])

    def tload_tm(tile, dram_h, store_rv):
        vs = None
        for s in range(NS):
            ws = [store_rv] + ([Rres.get("tmb")] if s == 0 else [])
            vs = em.dmat("scalar", "dtm", tmb[:, s * C:(s + 1) * C],
                         dram_h.ap()[:, tile * TT + s * 128: tile * TT + (s + 1) * 128],
                         ws)
        Rres["tmb"] = None
        return vs

    def emit_resid(loads_rv):
        vs = None
        for s in range(NS):
            em.wr("vector", loads_rv)
            vs = em.i("vector", lambda e, s=s: e.tensor_tensor(
                out=x_sb[:, s * C:(s + 1) * C], in0=x_sb[:, s * C:(s + 1) * C],
                in1=tmb[:, s * C:(s + 1) * C], op=ALU.add), "sdve")
        Rres["tmb"] = None
        return vs

    def emit_ffn(tile, fm_ready_dsp, bg=False, mid=None, post_gu=None):
        wg_r = wg_h.ap().rearrange("(cc p) i -> p cc i", p=128)
        wu_r = wu_h.ap().rearrange("(cc p) i -> p cc i", p=128)
        load_g = {}
        load_u = {}
        mm_g = {}
        mm_u = {}
        ev_g = {}
        ev_u = {}
        mul_f = {}
        for ic in range(ICN):
            half = (ic // 2) % 2
            if ic % 2 == 0:
                waits = [Rres.get(f"WAh{half}")]
                if ic < 4:
                    waits += [Rres.get(f"WDq{q}") for q in range(4)] + [Rres.get("WA")]
                dstA = WA[:, half * 4096:(half + 1) * 4096].rearrange(
                    "p (cc i) -> p cc i", cc=CCN)
                load_g[ic] = em.dma("gpsimd", f"dg{half}", dstA,
                                    wg_r[:, :, ic * 128:(ic + 2) * 128], waits)
                waits = [Rres.get(f"WBh{half}")]
                if ic < 4:
                    waits += [Rres.get(f"WDq{q}") for q in range(4)] + [Rres.get("WB")]
                dstB = WB[:, half * 4096:(half + 1) * 4096].rearrange(
                    "p (cc i) -> p cc i", cc=CCN)
                load_u[ic] = em.dma("gpsimd", f"du{half}", dstB,
                                    wu_r[:, :, ic * 128:(ic + 2) * 128], waits)
                if bg:
                    feed_bg(2)
            pg = pp[ic % 2]
            em.w("tensor", f"dg{half}", load_g[ic - ic % 2])
            if ic == 0:
                em.w("tensor", "dfm0", fm_ready_dsp[0])
            em.wr("tensor", Rpsum[ic % 2])
            for cc in range(CCN):
                lhs = WA[:, half * 4096 + cc * 256 + (ic % 2) * 128:
                         half * 4096 + cc * 256 + (ic % 2) * 128 + 128]
                em.i("tensor", lambda e, p=pg, l=lhs, cc=cc: e.matmul(
                    p[:], l, fmb[:, cc * TT:(cc + 1) * TT],
                    start=(cc == 0), stop=(cc == CCN - 1)),
                    "spe" if cc == CCN - 1 else None)
            mm_g[ic] = em.cnt["spe"]
            Rres[f"WAh{half}"] = ("spe", mm_g[ic])
            pu = pp[2 + ic % 2]
            em.w("tensor", f"du{half}", load_u[ic - ic % 2])
            em.wr("tensor", Rpsum[2 + ic % 2])
            for cc in range(CCN):
                lhs = WB[:, half * 4096 + cc * 256 + (ic % 2) * 128:
                         half * 4096 + cc * 256 + (ic % 2) * 128 + 128]
                em.i("tensor", lambda e, p=pu, l=lhs, cc=cc: e.matmul(
                    p[:], l, fmb[:, cc * TT:(cc + 1) * TT],
                    start=(cc == 0), stop=(cc == CCN - 1)),
                    "spe" if cc == CCN - 1 else None)
            mm_u[ic] = em.cnt["spe"]
            Rres[f"WBh{half}"] = ("spe", mm_u[ic])
            slot = ic % 2
            em.w("scalar", "spe", mm_g[ic])
            if ic == 0:
                em.w("scalar", "dsg", v_sg)
                em.w("scalar", "dsu", v_su)
            em.wr("scalar", Rres.get(f"gs{slot}"))
            ev_g[ic] = em.i("scalar", lambda e, p=pg, slot=slot, ic=ic: e.activation(
                out=gs[:, slot * TT:(slot + 1) * TT], in_=p[:], func=AF.Silu,
                scale=sg_sb[:, ic:ic + 1]), "sact")
            em.w("scalar", "spe", mm_u[ic])
            ev_u[ic] = em.i("scalar", lambda e, p=pu, slot=slot, ic=ic: e.activation(
                out=us[:, slot * TT:(slot + 1) * TT], in_=p[:], func=AF.Copy,
                scale=su_sb[:, ic:ic + 1]), "sact")
            Rpsum[ic % 2] = ("sact", ev_g[ic])
            Rpsum[2 + ic % 2] = ("sact", ev_u[ic])
            em.w("vector", "sact", ev_u[ic])
            if ic == 0:
                em.wr("vector", Rres.get("fuse"))
            mul_f[ic] = em.i("vector", lambda e, slot=slot, ic=ic: e.tensor_tensor(
                out=fuse[:, ic * TT:(ic + 1) * TT], in0=gs[:, slot * TT:(slot + 1) * TT],
                in1=us[:, slot * TT:(slot + 1) * TT], op=ALU.mult), "sdve")
            Rres[f"gs{slot}"] = ("sdve", mul_f[ic])
        fuse_all = mul_f[ICN - 1]
        Rres["fmb"] = ("spe", mm_u[ICN - 1])
        if post_gu is not None:
            post_gu()

        v_last_mm = None
        v_last_ev = None
        wd_r2 = wd_h.ap().rearrange("(ic r p) c -> ic p r c", p=128, r=2)
        for pr in range(2):
            mm_ic = {}
            ld_d = {}
            for ic in range(ICN):
                slot = (ic // 2) % 4
                WS = WA if slot < 2 else WB
                ho = slot % 2
                if ic % 2 == 0:
                    waits = [Rres.get(f"WDq{slot}")]
                    if ic - 7 in mm_ic:
                        waits.append(("spe", mm_ic[ic - 7]))
                    if pr == 0 and ic < 8:
                        if slot < 2:
                            waits += [Rres.get(f"WAh{ho}"), Rres.get("WA")]
                        else:
                            waits += [Rres.get(f"WBh{ho}"), Rres.get("WB")]
                    dstd = WS[:, ho * 4096:(ho + 1) * 4096].rearrange(
                        "p (r c) -> p r c", r=2)
                    ld_d[ic] = em.dma("gpsimd", f"dd{slot}", dstd, wd_r2[ic // 2], waits)
                em.w("tensor", f"dd{slot}", ld_d[ic - ic % 2])
                if ic == 0:
                    em.w("tensor", "sdve", fuse_all)
                    for b in range(8):
                        em.wr("tensor", Rpsum[b])
                for s2 in range(2):
                    s = pr * 2 + s2
                    lhs = fuse[:, ic * TT + s * 128: ic * TT + (s + 1) * 128]
                    for ct in range(4):
                        em.i("tensor", lambda e, b=s2 * 4 + ct, l=lhs, W=WS, ho=ho,
                             r=ic % 2, ct=ct, ic=ic: e.matmul(
                                 pp[b][:], l,
                                 W[:, ho * 4096 + r * 2048 + ct * 512:
                                   ho * 4096 + r * 2048 + (ct + 1) * 512],
                                 start=(ic == 0), stop=(ic == ICN - 1)),
                             "spe" if (s2 == 1 and ct == 3) else None)
                mm_ic[ic] = em.cnt["spe"]
            v_last_mm = em.cnt["spe"]
            for slot in range(4):
                Rres[f"WDq{slot}"] = ("spe", v_last_mm)
            em.w("vector", "spe", v_last_mm)
            for s2 in range(2):
                s = pr * 2 + s2
                for ct in range(4):
                    b = s2 * 4 + ct
                    em.w("vector", "sdve", em.cnt["sdve"])
                    em.i("vector", lambda e, b=b, ct=ct: e.tensor_tensor(
                        out=dtmp[:], in0=pp[b][:],
                        in1=sdh_b[:, ct * 512:(ct + 1) * 512], op=ALU.mult), "sdve")
                    em.w("vector", "sdve", em.cnt["sdve"])
                    v_last_ev = em.i("vector", lambda e, s=s, ct=ct: e.tensor_tensor(
                        out=x_sb[:, s * C + ct * 512: s * C + (ct + 1) * 512],
                        in0=x_sb[:, s * C + ct * 512: s * C + (ct + 1) * 512],
                        in1=dtmp[:], op=ALU.add), "sdve")
                    Rpsum[b] = ("sdve", v_last_ev)
            if pr == 0 and mid is not None:
                mid()
        Rres["WA"] = ("spe", v_last_mm)
        Rres["WB"] = ("spe", v_last_mm)
        for hh in range(2):
            Rres[f"WAh{hh}"] = ("spe", v_last_mm)
            Rres[f"WBh{hh}"] = ("spe", v_last_mm)
        Rres["fuse"] = ("spe", v_last_mm)
        return v_last_ev

    xp_store = {}
    x_dve_done = {}

    svA1 = {}

    def emit_ffnA_prefix(t):
        if t not in xload:
            waits = []
            if t - 1 in xp_store:
                waits.append(("dxp", xp_store[t - 1]))
                waits.append(("sdve", x_dve_done[t - 1]))
            src = x_ap.rearrange("(t s p) c -> t p s c", p=128, s=NS)[t]
            dst = x_sb[:].rearrange("p (s c) -> p s c", s=NS)
            xload[t] = em.dma("scalar", "dx", dst, src, waits)
        vA = emit_ln(("dx", xload[t]))
        svA1[t] = store_tm(t, yA_d, vA)

    def emit_pass1_ffnA(t, post_gu=None):
        if t not in svA1:
            emit_ffnA_prefix(t)
        fm_v = tload_fm(t, yA_d, svA1[t])
        vres = emit_ffn(t, fm_v, bg=(t in (0, 1)),
                        mid=lambda: emit_ln_stats(("sdve", em.cnt["sdve"]), [0, 1]),
                        post_gu=post_gu)
        vB = emit_ln(("sdve", vres), subs=[2, 3])
        svB = store_tm(t, yB_d, vB)
        x_dve_done[t] = vB
        xp_dst = xp_d.ap().rearrange("(t s p) c -> t p s c", p=128, s=NS)[t]
        xp_store[t] = em.dma("sync", "dxp", xp_dst,
                             x_sb[:].rearrange("p (s c) -> p s c", s=NS),
                             [("sdve", vres)])
        if stage == 1:
            o_dst = out_ap.rearrange("(t s p) c -> t p s c", p=128, s=NS)[t]
            xp_store[t] = em.dma("sync", "dxp", o_dst,
                                 x_sb[:].rearrange("p (s c) -> p s c", s=NS),
                                 [("sdve", vres)])
        return svB

    def emit_pass1_qkv(t, svB, fm_v=None):
        if fm_v is None:
            fm_v = tload_fm(t, yB_d, svB)
        wqkvT_r = wqkvT.ap().rearrange("(cc p) o -> p cc o", p=128)
        mm_q = {}
        st_q = {}
        for oc in range(CCN):
            slot = oc % 2
            ld = em.dma("sync", f"dwc{slot}",
                        WC[slot][:].rearrange("p (cc o) -> p cc o", cc=CCN),
                        wqkvT_r[:, :, oc * 128:(oc + 1) * 128],
                        [("dp0t", ph0["T_done"]), Rres.get(f"WC{slot}")])
            pq = pp[oc % 2]
            em.w("tensor", f"dwc{slot}", ld)
            if oc == 0:
                em.w("tensor", "dfm0", fm_v[0])
            em.wr("tensor", Rpsum[oc % 2])
            for cc in range(CCN):
                em.i("tensor", lambda e, p=pq, W=WC[slot], cc=cc: e.matmul(
                    p[:], W[:, cc * 128:(cc + 1) * 128], fmb[:, cc * TT:(cc + 1) * TT],
                    start=(cc == 0), stop=(cc == CCN - 1)),
                    "spe" if cc == CCN - 1 else None)
            mm_q[oc] = em.cnt["spe"]
            Rres[f"WC{slot}"] = ("spe", mm_q[oc])
            em.w("scalar", "spe", mm_q[oc])
            em.wr("scalar", Rres.get(f"qstore{slot}"))
            ev = em.i("scalar", lambda e, p=pq, slot=slot: e.activation(
                out=qstore[:, slot * TT:(slot + 1) * TT], in_=p[:], func=AF.Relu), "sact")
            Rpsum[oc % 2] = ("sact", ev)
            st_q[oc] = em.dma("scalar", f"dqs{slot}",
                              q_d.ap()[oc * 128:(oc + 1) * 128, t * TT:(t + 1) * TT],
                              qstore[:, slot * TT:(slot + 1) * TT],
                              [("sact", ev)])
            Rres[f"qstore{slot}"] = (f"dqs{slot}", st_q[oc])
        vk_ev_last = None
        for j in range(4):
            ldk = em.dma("sync", "dk", WA[:].rearrange("p (cc o) -> p cc o", cc=CCN),
                         wqkvT_r[:, :, C + j * TT:C + (j + 1) * TT],
                         [("dp0t", ph0["T_done"]), Rres.get("WAh0"), Rres.get("WAh1")] +
                         [Rres.get(f"WDq{q}") for q in range(4)] + [Rres.get("WA")])
            ldv = em.dma("sync", "dv", WB[:].rearrange("p (cc o) -> p cc o", cc=CCN),
                         wqkvT_r[:, :, 2 * C + j * TT:2 * C + (j + 1) * TT],
                         [("dp0t", ph0["T_done"]), Rres.get("WBh0"), Rres.get("WBh1")] +
                         [Rres.get(f"WDq{q}") for q in range(4)] + [Rres.get("WB")])
            mm_k = {}
            mm_v = {}
            ev_k = {}
            ev_v = {}
            for s in range(NS):
                pk = pp[2 + s % 2]
                em.w("tensor", "dk", ldk)
                if (j, s) == (0, 0):
                    em.w("tensor", "dfm0", fm_v[0])
                em.wr("tensor", Rpsum[2 + s % 2])
                for cc in range(CCN):
                    em.i("tensor", lambda e, p=pk, cc=cc, s=s: e.matmul(
                        p[:], fmb[:, cc * TT + s * 128: cc * TT + (s + 1) * 128],
                        WA[:, cc * TT:(cc + 1) * TT],
                        start=(cc == 0), stop=(cc == CCN - 1)),
                        "spe" if cc == CCN - 1 else None)
                mm_k[s] = em.cnt["spe"]
                em.w("scalar", "spe", mm_k[s])
                em.wr("scalar", Rres.get("k_sb"))
                ev_k[s] = em.i("scalar", lambda e, p=pk, s=s: e.activation(
                    out=k_sb[:, s * TT:(s + 1) * TT], in_=p[:], func=AF.Relu), "sact")
                Rpsum[2 + s % 2] = ("sact", ev_k[s])
                pv = pp[4 + s % 2]
                em.w("tensor", "dv", ldv)
                em.wr("tensor", Rpsum[4 + s % 2])
                for cc in range(CCN):
                    em.i("tensor", lambda e, p=pv, cc=cc, s=s: e.matmul(
                        p[:], fmb[:, cc * TT + s * 128: cc * TT + (s + 1) * 128],
                        WB[:, cc * TT:(cc + 1) * TT],
                        start=(cc == 0), stop=(cc == CCN - 1)),
                        "spe" if cc == CCN - 1 else None)
                mm_v[s] = em.cnt["spe"]
                em.w("vector", "spe", mm_v[s])
                em.wr("vector", Rres.get("v_sb"))
                if (j, s) == (0, 0):
                    em.w("vector", "sdve", v_ones)
                vdst = v_sb[:, s * 528:(s + 1) * 528].rearrange(
                    "p (h e) -> p h e", h=16)[:, :, 0:32]
                vsrc = pv[:].rearrange("p (h e) -> p h e", e=32)
                ev_v[s] = em.i("vector", lambda e, d=vdst, sr=vsrc: e.tensor_copy(d, sr),
                               "sdve")
                Rpsum[4 + s % 2] = ("sdve", ev_v[s])
            Rres["WA"] = ("spe", mm_k[NS - 1])
            Rres["WB"] = ("spe", mm_v[NS - 1])
            for ql in range(4):
                qg = 4 * j + ql
                b = 6 + ql % 2
                pvx = pp[b]
                em.w("tensor", "sact", ev_k[NS - 1])
                em.w("tensor", "sdve", ev_v[NS - 1])
                em.wr("tensor", Rpsum[b])
                for s in range(NS):
                    em.i("tensor", lambda e, p=pvx, s=s, ql=ql: e.matmul(
                        p[:, 0:132], k_sb[:, s * TT + ql * 128: s * TT + (ql + 1) * 128],
                        v_sb[:, s * 528 + ql * 132: s * 528 + (ql + 1) * 132],
                        start=(s == 0), stop=(s == NS - 1)),
                        "spe" if s == NS - 1 else None)
                v_mm = em.cnt["spe"]
                em.w("vector", "spe", v_mm)
                vk_ev_last = em.i("vector", lambda e, p=pvx, qg=qg: e.tensor_tensor(
                    out=vk_acc[:, qg * 132:(qg + 1) * 132],
                    in0=vk_acc[:, qg * 132:(qg + 1) * 132],
                    in1=p[:, 0:132], op=ALU.add), "sdve")
                Rpsum[b] = ("sdve", vk_ev_last)
                Rres["k_sb"] = ("spe", v_mm)
                Rres["v_sb"] = ("spe", v_mm)
        Rres["fmb"] = ("spe", em.cnt["spe"])
        return vk_ev_last

    def emit_collective(v_vk_last):
        v_st = em.dma("scalar", "dvk", vk_in.ap(), vk_acc[:], [("sdve", v_vk_last)])
        em.w("gpsimd", "dvk", v_st)
        em.i("gpsimd", lambda e: e.collective_compute(
            "AllReduce", ALU.add,
            replica_groups=[[0, 1], [2, 3], [4, 5], [6, 7]],
            ins=[vk_in.ap().opt()], outs=[vk_out.ap().opt()]), "scc", 1)
        em.w("gpsimd", "scc", em.cnt["scc"])
        vs = None
        for a in range(4):
            src = vk_out.ap()[32 * a:32 * (a + 1), :].rearrange(
                "p (q e) -> p q e", q=16)[:, :, 33 * a:33 * a + 33]
            dst = vkbd[32 * a:32 * (a + 1), :].rearrange(
                "p (q e) -> p q e", q=16)[:, :, 33 * a:33 * a + 33]
            vs = em.dma("gpsimd", "dvkbd", dst, src)
        return vs

    out_store = {}
    sv_att3 = {}
    svC3 = {}

    def emit_p3_attn(t, v_vkbd):
        qsrc = q_d.ap().rearrange("(oc p) tt -> p oc tt", p=128)[:, :, t * TT:(t + 1) * TT]
        qdst = fmb[:].rearrange("p (oc tt) -> p oc tt", oc=CCN)
        vq = em.dma("scalar", "dfmq", qdst, qsrc,
                    [("dqs0", st_q_last[t][0]), ("dqs1", st_q_last[t][1]),
                     Rres.get("fmb")])
        att_raw = v_sb
        att_mm_last = None
        sv_att = None
        sv_atts = {}
        for s in range(NS):
            cps = {}
            for oc in range(CCN):
                idx = s * CCN + oc
                b = idx % 6
                pvx = pp[b]
                em.w("tensor", "dfmq", vq)
                if t == 0 and idx == 0:
                    em.w("tensor", "dvkbd", v_vkbd)
                em.wr("tensor", Rpsum[b])
                em.i("tensor", lambda e, p=pvx, s=s, oc=oc: e.matmul(
                    p[:, 0:132], fmb[:, oc * TT + s * 128: oc * TT + (s + 1) * 128],
                    vkbd[:, oc * 132:(oc + 1) * 132], start=True, stop=True), "spe")
                att_mm_last = em.cnt["spe"]
                em.w("scalar", "spe", att_mm_last)
                if s == 0 and oc == 0:
                    em.wr("scalar", Rres.get("v_sb"))
                cps[oc] = em.i("scalar", lambda e, p=pvx, oc=oc: e.activation(
                    out=att_raw[:, oc * 132:(oc + 1) * 132], in_=p[:, 0:132],
                    func=AF.Copy), "sact")
                Rpsum[b] = ("sact", cps[oc])
            em.w("vector", "sact", cps[CCN - 1])
            if s == 0:
                em.wr("vector", Rres.get("tmb"))
                em.wr("vector", Rres.get("tmb_1"))
            em.w("vector", "sdve", em.cnt["sdve"])
            em.i("vector", lambda e: e.tensor_scalar(
                out=rc64[:], in0=att_raw[:, 32::33], scalar1=ATT_EPS, scalar2=None,
                op0=ALU.add), "sdve")
            em.w("vector", "sdve", em.cnt["sdve"])
            em.i("vector", lambda e: e.reciprocal(rc64[:], rc64[:]), "sdve")
            em.w("vector", "sdve", em.cnt["sdve"])
            num = att_raw[:].rearrange("p (g e) -> p g e", g=CCN * 4)[:, :, 0:32]
            den = rc64[:].rearrange("p (g one) -> p g one", one=1)
            num_b, den_b = bass.broadcast_tensor_aps(num, den)
            dstv = tmb[:, s * C:(s + 1) * C].rearrange("p (g e) -> p g e", g=CCN * 4)
            div_v = em.i("vector", lambda e, d=dstv, n=num_b, r=den_b: e.tensor_tensor(
                out=d, in0=n, in1=r, op=ALU.mult), "sdve")
            Rres["v_sb"] = ("sdve", div_v)
            dst = att_d.ap().rearrange("(t s p) c -> t s p c", p=128, s=NS)[t][s]
            sv_att = em.dma("scalar", "dty0", dst, tmb[:, s * C:(s + 1) * C],
                            [("sdve", div_v)])
            sv_atts[0] = sv_att
            sv_atts[1] = 0
        Rres["tmb"] = ("dty0", sv_atts[0])
        Rres["tmb_1"] = None
        Rres["fmb"] = ("spe", att_mm_last)
        sv_att3[t] = (sv_atts[0], sv_atts[1])

    def emit_p3_projln(t):
        if t == 0:
            waits = [("dxp", xp_store[NT - 1]), ("sdve", x_dve_done[NT - 1])]
        else:
            waits = [("dxp", max(xp_store[t], out_store[t - 1]))]
        srcx = xp_d.ap().rearrange("(t s p) c -> t p s c", p=128, s=NS)[t]
        dstx = x_sb[:].rearrange("p (s c) -> p s c", s=NS)
        vx = em.dma("scalar", "dx", dstx, srcx, waits)
        fm_v = tload_fm(t, att_d, sv_att3[t])
        wprojT_r2 = wprojT.ap().rearrange("(oc p) o2 -> oc p o2", p=128)
        v_last_mm = None
        vres2 = None
        for pr in range(2):
            mm_oc = {}
            ld_p = {}
            for oc in range(CCN):
                slot = oc % 2
                waits = [Rres.get(f"WPq{slot}")]
                if oc - 2 in mm_oc:
                    waits.append(("spe", mm_oc[oc - 2]))
                if pr == 0 and oc < 2:
                    waits.append(("dp0t", ph0["T_done"]))
                    waits.append(Rres.get(f"WC{slot}"))
                ld_p[oc] = em.dma("sync", f"dwc{slot}", WC[slot][:], wprojT_r2[oc], waits)
                em.w("tensor", f"dwc{slot}", ld_p[oc])
                if oc == 0:
                    em.w("tensor", "dfm0", fm_v[0])
                    em.w("vector", "dx", vx)
                    for b in range(8):
                        em.wr("tensor", Rpsum[b])
                for s2 in range(2):
                    s = pr * 2 + s2
                    lhs = fmb[:, oc * TT + s * 128: oc * TT + (s + 1) * 128]
                    for ct in range(4):
                        em.i("tensor", lambda e, b=s2 * 4 + ct, l=lhs, slot=slot, ct=ct,
                             oc=oc: e.matmul(
                                 pp[b][:], l,
                                 WC[slot][:, ct * 512:(ct + 1) * 512],
                                 start=(oc == 0), stop=(oc == CCN - 1)),
                             "spe" if (s2 == 1 and ct == 3) else None)
                mm_oc[oc] = em.cnt["spe"]
            v_last_mm = em.cnt["spe"]
            for slot in range(2):
                Rres[f"WPq{slot}"] = ("spe", v_last_mm)
            em.w("vector", "spe", v_last_mm)
            for s2 in range(2):
                s = pr * 2 + s2
                for ct in range(4):
                    b = s2 * 4 + ct
                    em.w("vector", "sdve", em.cnt["sdve"])
                    em.i("vector", lambda e, b=b, ct=ct: e.tensor_tensor(
                        out=dtmp[:], in0=pp[b][:],
                        in1=bp_b[:, ct * 512:(ct + 1) * 512], op=ALU.add), "sdve")
                    em.w("vector", "sdve", em.cnt["sdve"])
                    vres2 = em.i("vector", lambda e, s=s, ct=ct: e.tensor_tensor(
                        out=x_sb[:, s * C + ct * 512: s * C + (ct + 1) * 512],
                        in0=x_sb[:, s * C + ct * 512: s * C + (ct + 1) * 512],
                        in1=dtmp[:], op=ALU.add), "sdve")
                    Rpsum[b] = ("sdve", vres2)
            if pr == 0:
                emit_ln_stats(("sdve", em.cnt["sdve"]), [0, 1])
        Rres["WC0"] = ("spe", v_last_mm)
        Rres["WC1"] = ("spe", v_last_mm)
        Rres["fmb"] = ("spe", v_last_mm)
        if stage == 2:
            o_dst = out_ap.rearrange("(t s p) c -> t p s c", p=128, s=NS)[t]
            out_store[t] = em.dma("sync", "dxp", o_dst,
                                  x_sb[:].rearrange("p (s c) -> p s c", s=NS),
                                  [("sdve", vres2)])
            return
        vC = emit_ln(("sdve", vres2), subs=[2, 3])
        svC3[t] = store_tm(t, yC_d, vC)

    def emit_p3_ffnC(t):
        fm_v2 = tload_fm(t, yC_d, svC3[t])
        vres3 = emit_ffn(t, fm_v2)
        o_dst = out_ap.rearrange("(t s p) c -> t p s c", p=128, s=NS)[t]
        out_store[t] = em.dma("sync", "dxp", o_dst,
                              x_sb[:].rearrange("p (s c) -> p s c", s=NS),
                              [("sdve", vres3)])

    st_q_last = {}
    vk_last = None
    svB = {}
    fmq = {}

    def _mk_post_gu(tq):
        def h():
            fmq[tq] = tload_fm(tq, yB_d, svB[tq])
        return h

    svB[0] = emit_pass1_ffnA(0)
    svB[1] = emit_pass1_ffnA(1, post_gu=_mk_post_gu(0))
    feed_bg(len(ph0_jobs))
    for t in range(NT):
        if t + 2 < NT:
            emit_ffnA_prefix(t + 2)
        vk_last = emit_pass1_qkv(t, svB[t], fm_v=fmq.get(t))
        st_q_last[t] = (em.cnt["dqs0"], em.cnt["dqs1"])
        if t + 2 < NT:
            svB[t + 2] = emit_pass1_ffnA(t + 2, post_gu=_mk_post_gu(t + 1))
    if stage >= 2 and stage != 11:
        v_vkbd = emit_collective(vk_last)
        if stage == 2:
            for t in range(NT):
                emit_p3_attn(t, v_vkbd)
                emit_p3_projln(t)
        else:
            emit_p3_attn(0, v_vkbd)
            emit_p3_attn(1, v_vkbd)
            emit_p3_projln(0)
            emit_p3_attn(2, v_vkbd)
            emit_p3_ffnC(0)
            emit_p3_projln(1)
            emit_p3_attn(3, v_vkbd)
            emit_p3_ffnC(1)
            emit_p3_projln(2)
            emit_p3_ffnC(2)
            emit_p3_projln(3)
            emit_p3_ffnC(3)
        final_v = out_store[NT - 1]
    else:
        final_v = xp_store[NT - 1]
    em.w("sync", "dxp", final_v)

    with nc.Block() as block:
        @block.tensor
        def _(e):
            for f in em.streams["tensor"]:
                f(e)

        @block.scalar
        def _(e):
            for f in em.streams["scalar"]:
                f(e)

        @block.vector
        def _(e):
            for f in em.streams["vector"]:
                f(e)

        @block.gpsimd
        def _(e):
            for f in em.streams["gpsimd"]:
                f(e)

        @block.sync
        def _(e):
            for f in em.streams["sync"]:
                f(e)

    st.close()
    return nc


_NC_CACHE = {}


def kernel(L0_attn_x, L0_attn_w_qkv, L0_attn_w_proj, L0_attn_b_proj,
           L0_ffn_w_gate, L0_ffn_w_up, L0_ffn_w_down,
           L0_ffn_s_gate, L0_ffn_s_up, L0_ffn_s_down, _stage=3):
    x = np.ascontiguousarray(np.asarray(L0_attn_x, dtype=np.float32)).reshape(NCORES * TC, C)
    wqkv = np.ascontiguousarray(np.asarray(L0_attn_w_qkv, dtype=np.float32))
    wproj = np.ascontiguousarray(np.asarray(L0_attn_w_proj, dtype=np.float32))
    bproj = np.ascontiguousarray(np.asarray(L0_attn_b_proj, dtype=np.float32))
    wg = np.ascontiguousarray(np.asarray(L0_ffn_w_gate, dtype=np.int8))
    wu = np.ascontiguousarray(np.asarray(L0_ffn_w_up, dtype=np.int8))
    wd = np.ascontiguousarray(np.asarray(L0_ffn_w_down, dtype=np.int8))
    sg = np.ascontiguousarray(np.asarray(L0_ffn_s_gate, dtype=np.float32))
    su = np.ascontiguousarray(np.asarray(L0_ffn_s_up, dtype=np.float32))
    sd = np.ascontiguousarray(np.asarray(L0_ffn_s_down, dtype=np.float32))

    if _stage not in _NC_CACHE:
        _NC_CACHE[_stage] = build(stage=_stage)
    nc = _NC_CACHE[_stage]

    in_maps = []
    for c in range(NCORES):
        in_maps.append({
            "x": x[c * TC:(c + 1) * TC],
            "wqkv": wqkv, "wproj": wproj, "bproj": bproj,
            "wg": wg, "wu": wu, "wd": wd,
            "sg": sg, "su": su, "sd": sd,
        })
    res = run_bass_kernel_spmd(nc, in_maps, core_ids=list(range(NCORES)))
    if _stage == 11:
        return res.results
    out = np.concatenate([np.asarray(res.results[c]["out"]) for c in range(NCORES)], axis=0)
    return out.reshape(4, 4096, 2048)
